# version 04
# baseline (speedup 1.0000x reference)
import numpy as np
import ml_dtypes
import concourse.bass as bass
import concourse.mybir as mybir
from concourse.bass_utils import run_bass_kernel_spmd

F32 = mybir.dt.float32
BF16 = mybir.dt.bfloat16
I32 = mybir.dt.int32
AF = mybir.ActivationFunctionType
ALU = mybir.AluOpType
AX = mybir.AxisListType.X

D = 1024
NCH = 8
TLOC = 4096
TT = 512
NTT = TLOC // TT
H = 16
DH = 64
FF = 4096
NEXP = 8
LCH = 128
EPS_LN = 1e-5
EPS_Z = 1e-6
GAMMA = [1.0 - 2.0 ** (-5.0 - i) for i in range(H)]


ENG_NAMES = ("sp", "act", "dve", "pe", "pl")


class _Op:
    __slots__ = ("eng", "emit", "sem", "inc", "deps", "need_inc", "cum",
                 "is_async", "waits", "tag", "idx")

    def __init__(self, eng, emit, sem, inc, is_async, tag, idx):
        self.eng, self.emit, self.sem, self.inc = eng, emit, sem, inc
        self.is_async, self.tag, self.idx = is_async, tag, idx
        self.deps = set()
        self.need_inc = False
        self.cum = None
        self.waits = None


class Builder:
    CSEM = {"pe": "pe_c", "dve": "dve_c", "act": "act_c", "pl": "pl_c", "sp": "sp_c"}
    DSEM = {"sp": "sp_d", "act": "act_d", "pl": "pl_d"}

    def __init__(self):
        self.ops = []
        self.phases = []
        self.state = {}
        self.sem_names = set()

    def phase(self, name):
        self.phases.append((name, []))

    def _key(self, buf):
        t = getattr(buf, "tensor", buf)
        return getattr(t, "name", None) or id(t)

    def _add(self, eng, emit, reads, writes, sem, inc, is_async, tag):
        op = _Op(eng, emit, sem, inc, is_async, tag, len(self.ops))
        self.sem_names.add(sem)
        for bf in reads:
            st = self.state.get(self._key(bf))
            if st is not None and st[0] is not None:
                op.deps.add(st[0])
        for bf in writes:
            st = self.state.get(self._key(bf))
            if st is not None:
                if st[0] is not None:
                    op.deps.add(st[0])
                op.deps.update(st[1])
        for bf in reads:
            self.state.setdefault(self._key(bf), [None, []])[1].append(op)
        for bf in writes:
            self.state[self._key(bf)] = [op, []]
        op.deps.discard(op)
        self.ops.append(op)
        self.phases[-1][1].append(op)
        return op

    def op(self, eng, emit, reads=(), writes=(), tag=""):
        return self._add(eng, emit, reads, writes, self.CSEM[eng], 1, False, tag)

    def dma(self, eng, out, in_, reads=(), writes=(), tag="", **kw):
        emit = (lambda e, _o=out, _i=in_, _kw=kw: e.dma_start(_o, _i, **_kw))
        return self._add(eng, emit, reads, writes, self.DSEM[eng], 16, True, tag)

    def cc(self, emit, reads=(), writes=(), tag=""):
        return self._add("pl", emit, reads, writes, "cc", 1, True, tag)

    SAFE_SAME_ENGINE = True

    def finalize_deps(self):
        for op in self.ops:
            w = []
            for d in op.deps:
                if (not d.is_async) and (not op.is_async) and d.eng == op.eng:
                    if d.eng == "pe" or not self.SAFE_SAME_ENGINE:
                        continue
                if (not d.is_async) and op.is_async and d.eng == op.eng:
                    pass
                d.need_inc = True
                w.append(d)
            op.waits = w
        for op in self.ops:
            if op.is_async:
                op.need_inc = True
        cum = {}
        for op in self.ops:
            if op.need_inc:
                cum[op.sem] = cum.get(op.sem, 0) + op.inc
                op.cum = cum[op.sem]
        self.final_counts = dict(cum)

    def render(self, nc):
        sems = {}
        from contextlib import ExitStack
        self._stack = ExitStack()
        for sname in sorted(self.sem_names):
            sems[sname] = self._stack.enter_context(nc.semaphore(sname))
        self.sems = sems
        watermark = {e: {} for e in ENG_NAMES}

        def emit_stream(engobj, engname, ops):
            wm = watermark[engname]
            for op in ops:
                tgt = {}
                for d in op.waits:
                    if d.cum is None:
                        continue
                    if tgt.get(d.sem, 0) < d.cum:
                        tgt[d.sem] = d.cum
                for sname, val in tgt.items():
                    if wm.get(sname, 0) >= val:
                        continue
                    engobj.wait_ge(sems[sname], val)
                    wm[sname] = val
                inst = op.emit(engobj)
                if op.need_inc:
                    inst.then_inc(sems[op.sem], op.inc)

        for pname, pops in self.phases:
            by_eng = {e: [] for e in ENG_NAMES}
            for op in pops:
                by_eng[op.eng].append(op)
            with nc.Block() as blk:
                if by_eng["sp"]:
                    blk.sync(lambda e, _o=by_eng["sp"]: emit_stream(e, "sp", _o))
                if by_eng["act"]:
                    blk.scalar(lambda e, _o=by_eng["act"]: emit_stream(e, "act", _o))
                if by_eng["dve"]:
                    blk.vector(lambda e, _o=by_eng["dve"]: emit_stream(e, "dve", _o))
                if by_eng["pe"]:
                    blk.tensor(lambda e, _o=by_eng["pe"]: emit_stream(e, "pe", _o))
                if by_eng["pl"]:
                    blk.gpsimd(lambda e, _o=by_eng["pl"]: emit_stream(e, "pl", _o))

        with nc.Block() as blk:
            def _drain(e):
                for sname, tot in sorted(self.final_counts.items()):
                    if sname in ("sp_d", "act_d", "pl_d", "cc"):
                        e.wait_ge(sems[sname], tot)
            blk.gpsimd(_drain)



class Ctx:
    def __init__(self, nc, b, ext_in, ext_out, fused):
        self.nc, self.b = nc, b
        self.ext_in, self.ext_out = ext_in, ext_out
        self.fused = fused
        self.dram_t = {}
        self.sb_t = {}
        self.ps_t = {}

    def dram(self, name, shape, dtype):
        if name not in self.dram_t:
            if name in self.ext_in:
                kind = "ExternalInput"
            elif name in self.ext_out:
                kind = "ExternalOutput"
            else:
                kind = "Internal"
            self.dram_t[name] = self.nc.dram_tensor(name, list(shape), dtype, kind=kind)
        return self.dram_t[name]

    def sb(self, name, shape, dtype):
        if name not in self.sb_t:
            self.sb_t[name] = self.nc.alloc_sbuf_tensor(name, list(shape), dtype)
        return self.sb_t[name]

    NSLOT = 12

    def slot(self, i):
        name = "slot%d" % i
        if name not in self.sb_t:
            self.sb_t[name] = self.nc.alloc_sbuf_tensor(name, [128, 4096], F32)
        return self.sb_t[name]

    def ps(self, name, shape=(128, 512), dtype=F32):
        if name not in self.ps_t:
            self.ps_t[name] = self.nc.alloc_psum_tensor(name, list(shape), dtype)
        return self.ps_t[name]



def d_x(c):      return c.dram("x", (TLOC, D), F32)
def d_out(c):    return c.dram("o", (TLOC, D), F32)
def d_x0fm(c):   return c.dram("x0fm", (D, TLOC), F32)
def d_xnb(c):    return c.dram("xnb", (D, TLOC), BF16)
def d_g0(c):     return c.dram("g0", (D, TLOC), F32)
def d_x1fm(c):   return c.dram("x1fm", (D, TLOC), F32)
def d_xn1b(c):   return c.dram("xn1b", (D, TLOC), BF16)
def d_yloc(c):   return c.dram("yloc", (D, TLOC), F32)
def d_qtb(c):    return c.dram("qtb", (D, TLOC), BF16)
def d_yfin(c):   return c.dram("yfin", (D, TLOC), F32)
def d_onehot(c): return c.dram("onehotq", (NEXP, TLOC), BF16)
def d_wrow(c):   return c.dram("wrowq", (1, TLOC), F32)

def d_id32(c):   return c.dram("c_id32", (128, 128), F32)
def d_idb(c):    return c.dram("c_idb", (128, 128), BF16)
def d_blk1(c):   return c.dram("c_blk1", (128, 128), F32)
def d_repsel(c): return c.dram("c_repsel", (16, 1024), F32)
def d_sel(c):    return c.dram("c_sel", (8, 1024), F32)
def d_p2(c):     return c.dram("c_p2", (128, 128), F32)
def d_causal(c): return c.dram("c_causal", (128, 128), F32)
def d_zmask(c):  return c.dram("c_zmask", (128, 1), F32)
def d_ropeAq(c): return c.dram("c_ropeAq", (D, TLOC), F32)
def d_ropeBq(c): return c.dram("c_ropeBq", (D, TLOC), F32)
def d_ropeAk(c): return c.dram("c_ropeAk", (D, TLOC), F32)
def d_ropeBk(c): return c.dram("c_ropeBk", (D, TLOC), F32)

def d_kv_in(c):   return c.dram("kv_in", (64, H * 65), F32)
def d_kv_out(c):  return c.dram("kv_out", (64, H * 65), F32)
def d_kv_pair(c): return c.dram("kv_pair", (64, H * 65), F32)
def d_st_in(c):   return c.dram("st_in", (64, H * 64), F32)
def d_st_out(c):  return c.dram("st_out", (64, H * 64), F32)
def d_st_pair(c): return c.dram("st_pair", (64, H * 64), F32)
def d_halo_in(c): return c.dram("halo_in", (128, 8), F32)
def d_halo_out(c): return c.dram("halo_out", (128, 8), F32)
def d_halo_pair(c): return c.dram("halo_pair", (128, 8), F32)
def d_gn_in(c):   return c.dram("gn_in", (16, 2), F32)
def d_gn_out(c):  return c.dram("gn_out", (16, 2), F32)
def d_gn_pair(c): return c.dram("gn_pair", (16, 2), F32)


def fm_ap(dram_handle):
    return dram_handle.ap().rearrange("(c p) t -> p c t", p=128)



def ln_rows(c, tag, x_sb, tmp_sq, ps_s1, ps_s2, row_r, row_nmr, eps):
    b = c.b
    ones_sum = c.sb("ones_sum", (128, 1), F32)
    for ch in range(NCH):
        b.op("act", lambda e, s=ch: e.activation(
            tmp_sq[:, s * TT:(s + 1) * TT], x_sb[:, s * TT:(s + 1) * TT], AF.Square),
            reads=(x_sb,), writes=(tmp_sq,), tag=tag + "sq")
    for ch in range(NCH):
        b.op("pe", lambda e, s=ch: e.matmul(
            ps_s1[0:1, 0:TT], ones_sum[:], x_sb[:, s * TT:(s + 1) * TT],
            start=(s == 0), stop=(s == NCH - 1)),
            reads=(x_sb, ones_sum), writes=(ps_s1,), tag=tag + "s1")
    for ch in range(NCH):
        b.op("pe", lambda e, s=ch: e.matmul(
            ps_s2[0:1, 0:TT], ones_sum[:], tmp_sq[:, s * TT:(s + 1) * TT],
            start=(s == 0), stop=(s == NCH - 1)),
            reads=(tmp_sq, ones_sum), writes=(ps_s2,), tag=tag + "s2")
    b.op("dve", lambda e: e.tensor_scalar(row_nmr[:], ps_s1[0:1, 0:TT], 1.0 / D, None, ALU.mult),
         reads=(ps_s1,), writes=(row_nmr,), tag=tag + "m")
    b.op("dve", lambda e: e.tensor_tensor(row_r[:], row_nmr[:], row_nmr[:], ALU.mult),
         reads=(row_nmr,), writes=(row_r,), tag=tag + "m2")
    b.op("dve", lambda e: e.scalar_tensor_tensor(
        row_r[:], ps_s2[0:1, 0:TT], 1.0 / D, row_r[:], ALU.mult, ALU.subtract),
        reads=(ps_s2, row_r), writes=(row_r,), tag=tag + "var")
    b.op("dve", lambda e: e.tensor_scalar_add(row_r[:], row_r[:], float(eps)),
         reads=(row_r,), writes=(row_r,), tag=tag + "eps")
    b.op("act", lambda e: e.activation(row_r[:], row_r[:], AF.Ln),
         reads=(row_r,), writes=(row_r,), tag=tag + "ln")
    b.op("act", lambda e: e.activation(row_r[:], row_r[:], AF.Exp, scale=-0.5),
         reads=(row_r,), writes=(row_r,), tag=tag + "exp")
    b.op("dve", lambda e: e.scalar_tensor_tensor(
        row_nmr[:], row_nmr[:], -1.0, row_r[:], ALU.mult, ALU.mult),
        reads=(row_nmr, row_r), writes=(row_nmr,), tag=tag + "nmr")


def bcast_row(c, tag, row, ps, reads_extra=()):
    b = c.b
    ones_bc = c.sb("ones_bc", (1, 128), F32)
    b.op("pe", lambda e: e.matmul(ps[:, 0:TT], ones_bc[:], row[:], start=True, stop=True),
         reads=(row, ones_bc) + tuple(reads_extra), writes=(ps,), tag=tag)


def apply_ln(c, tag, x_sb, ps_r, ps_nmr, out_sb, out_off=0):
    b = c.b
    for ch in range(NCH):
        b.op("dve", lambda e, s=ch: e.tensor_tensor(
            out_sb[:, s * TT:(s + 1) * TT], x_sb[:, s * TT:(s + 1) * TT],
            ps_r[:, 0:TT], ALU.mult),
            reads=(x_sb, ps_r), writes=(out_sb,), tag=tag + "lnm")
    for ch in range(NCH):
        b.op("dve", lambda e, s=ch: e.tensor_tensor(
            out_sb[:, s * TT:(s + 1) * TT], out_sb[:, s * TT:(s + 1) * TT],
            ps_nmr[:, 0:TT], ALU.add),
            reads=(out_sb, ps_nmr), writes=(out_sb,), tag=tag + "lna")


def f32v(slot):
    return slot[:, 0:4096]


def bf16v(slot):
    return slot.ap().bitcast(BF16)



def ph0(c):
    b = c.b
    b.phase("ph0")
    x = d_x(c); x0fm = d_x0fm(c); xnb = d_xnb(c)
    onehotq = d_onehot(c); wrowq = d_wrow(c)
    wr = c.dram("L0_ffn_Wr", (D, NEXP), F32)
    id32 = d_id32(c)

    S = c.slot
    xtm = f32v(S(0))
    xfm = f32v(S(1))
    xnf = f32v(S(2))
    tmp_sq = f32v(S(3))
    xnb_sb = bf16v(S(4))[:, 0:4096]
    id32_sb = c.sb("id32_sb", (128, 128), F32)
    wr_sb = c.sb("p0_wr", (128, NCH * NEXP), F32)
    row_r = c.sb("row_r", (1, TT), F32)
    row_nmr = c.sb("row_nmr", (1, TT), F32)
    colbuf = c.sb("p0_col", (128, 8), F32)
    logits_sb = c.sb("p0_logits", (128, NEXP), F32)
    eq_sb = c.sb("p0_eq", (128, NEXP), F32)
    l2_sb = c.sb("p0_l2", (128, NEXP), F32)
    oh_st = [c.sb("p0_ohst%d" % k, (8, 128), BF16) for k in range(2)]
    w_st = [c.sb("p0_wst%d" % k, (1, 128), F32) for k in range(2)]

    ps_tr = c.ps("ps0"); ps_tr2 = c.ps("ps1")
    ps_s1 = c.ps("ps2"); ps_s2 = c.ps("ps3")
    ps_r = c.ps("ps4"); ps_nmr = c.ps("ps5")
    ps_log = c.ps("ps6"); ps_oh = c.ps("ps7")

    ones_sum = c.sb("ones_sum", (128, 1), F32)
    ones_bc = c.sb("ones_bc", (1, 128), F32)
    b.op("dve", lambda e: e.memset(ones_sum[:], 1.0), writes=(ones_sum,), tag="ones_s")
    b.op("dve", lambda e: e.memset(ones_bc[:], 1.0), writes=(ones_bc,), tag="ones_b")
    b.dma("sp", id32_sb[:], id32.ap(), writes=(id32_sb,), tag="id32")
    b.dma("sp", wr_sb[:].rearrange("p (c e) -> p c e", c=NCH),
          wr.ap().rearrange("(c p) e -> p c e", p=128),
          writes=(wr_sb,), tag="wr")

    for ti in range(NTT):
        t0 = ti * TT
        b.dma("sp", xtm.rearrange("p (s f) -> p s f", s=4),
              x.ap()[t0:t0 + TT, :].rearrange("(s p) f -> p s f", p=128),
              writes=(S(0),), tag="p0 ld%d" % ti)
        for i in range(4):
            pst = ps_tr if i % 2 == 0 else ps_tr2
            for ch in range(NCH):
                b.op("pe", lambda e, i=i, ch=ch, pst=pst: e.transpose(
                    pst[:, (ch % 4) * 128:(ch % 4) * 128 + 128],
                    xtm[:, i * D + ch * 128: i * D + ch * 128 + 128], id32_sb[:]),
                    reads=(S(0), id32_sb), writes=(pst,), tag="p0 tr")
                b.op("dve", lambda e, i=i, ch=ch, pst=pst: e.tensor_copy(
                    xfm[:, ch * TT + i * 128: ch * TT + i * 128 + 128],
                    pst[:, (ch % 4) * 128:(ch % 4) * 128 + 128]),
                    reads=(pst,), writes=(S(1),), tag="p0 ev")
        b.dma("sp", fm_ap(x0fm)[:, :, t0:t0 + TT],
              xfm.rearrange("p (c t) -> p c t", c=NCH),
              reads=(S(1),), writes=(x0fm,), tag="p0 st_xfm")
        ln_rows(c, "p0 ln ", xfm, tmp_sq, ps_s1, ps_s2, row_r, row_nmr, EPS_LN)
        bcast_row(c, "p0 bcr", row_r, ps_r)
        bcast_row(c, "p0 bcn", row_nmr, ps_nmr)
        apply_ln(c, "p0 ", xfm, ps_r, ps_nmr, xnf)
        for ch in range(NCH):
            b.op("dve", lambda e, s=ch: e.tensor_copy(
                xnb_sb[:, s * TT:(s + 1) * TT], xnf[:, s * TT:(s + 1) * TT]),
                reads=(S(2),), writes=(S(4),), tag="p0 cast")
        b.dma("sp", fm_ap(xnb)[:, :, t0:t0 + TT],
              xnb_sb.rearrange("p (c t) -> p c t", c=NCH),
              reads=(S(4),), writes=(xnb,), tag="p0 st_xnb")
        for i in range(4):
            k = i % 2
            for ch in range(NCH):
                b.op("pe", lambda e, i=i, s=ch: e.matmul(
                    ps_log[0:128, 0:NEXP],
                    xnf[:, s * TT + i * 128: s * TT + i * 128 + 128],
                    wr_sb[:, s * NEXP:(s + 1) * NEXP],
                    start=(s == 0), stop=(s == NCH - 1)),
                    reads=(S(2), wr_sb), writes=(ps_log,), tag="p0 rt")
            b.op("dve", lambda e: e.tensor_copy(logits_sb[:], ps_log[0:128, 0:NEXP]),
                 reads=(ps_log,), writes=(logits_sb,), tag="p0 lcp")
            if "dbg_logits" in c.ext_out:
                dbg = c.dram("dbg_logits", (TLOC, NEXP), F32)
                b.dma("sp", dbg.ap()[t0 + i * 128: t0 + i * 128 + 128, :], logits_sb[:],
                      reads=(logits_sb,), writes=(dbg,), tag="p0 dbg")
            b.op("dve", lambda e: e.reduce_max(colbuf[:, 0:1], logits_sb[:], axis=AX),
                 reads=(logits_sb,), writes=(colbuf,), tag="p0 v1")
            b.op("dve", lambda e: e.tensor_tensor(
                eq_sb[:], logits_sb[:], colbuf[:, 0:1].broadcast_to([128, NEXP]), ALU.is_ge),
                 reads=(logits_sb, colbuf), writes=(eq_sb,), tag="p0 eq")
            b.op("dve", lambda e: e.scalar_tensor_tensor(
                l2_sb[:], eq_sb[:], -1e30, logits_sb[:], ALU.mult, ALU.add),
                reads=(eq_sb, logits_sb), writes=(l2_sb,), tag="p0 l2")
            b.op("dve", lambda e: e.reduce_max(colbuf[:, 1:2], l2_sb[:], axis=AX),
                 reads=(l2_sb,), writes=(colbuf,), tag="p0 v2")
            b.op("dve", lambda e: e.tensor_tensor(colbuf[:, 2:3], colbuf[:, 0:1],
                                                  colbuf[:, 1:2], ALU.subtract),
                 reads=(colbuf,), writes=(colbuf,), tag="p0 gap")
            b.op("act", lambda e: e.activation(colbuf[:, 3:4], colbuf[:, 2:3], AF.Sigmoid),
                 reads=(colbuf,), writes=(colbuf,), tag="p0 w")
            if "dbg_eq" in c.ext_out:
                dbge = c.dram("dbg_eq", (TLOC, NEXP), F32)
                b.dma("sp", dbge.ap()[t0 + i * 128: t0 + i * 128 + 128, :], eq_sb[:],
                      reads=(eq_sb,), writes=(dbge,), tag="p0 dbge")
            b.op("pe", lambda e: e.transpose(ps_oh[0:8, 0:128], eq_sb[:], id32_sb[:]),
                 reads=(eq_sb, id32_sb), writes=(ps_oh,), tag="p0 ohT")
            b.op("dve", lambda e, k=k: e.tensor_copy(oh_st[k][:], ps_oh[0:8, 0:128]),
                 reads=(ps_oh,), writes=(oh_st[k],), tag="p0 ohcp")
            b.dma("sp", onehotq.ap()[:, t0 + i * 128: t0 + i * 128 + 128], oh_st[k][:],
                  reads=(oh_st[k],), writes=(onehotq,), tag="p0 st_oh")
            b.op("pe", lambda e: e.transpose(ps_oh[0:1, 128:256], colbuf[:, 3:4], id32_sb[:]),
                 reads=(colbuf, id32_sb), writes=(ps_oh,), tag="p0 wT")
            b.op("dve", lambda e, k=k: e.tensor_copy(w_st[k][:], ps_oh[0:1, 128:256]),
                 reads=(ps_oh,), writes=(w_st[k],), tag="p0 wcp")
            b.dma("sp", wrowq.ap()[:, t0 + i * 128: t0 + i * 128 + 128], w_st[k][:],
                  reads=(w_st[k],), writes=(wrowq,), tag="p0 st_w")



def load_col(c, name, dram_name, ps_bank):
    b = c.b
    key = "col_" + name
    if key in c.sb_t:
        return c.sb_t[key]
    vec = c.dram(dram_name, (D,), F32)
    row = c.sb("colrow_stage", (32, 128), F32)[0:8, :]
    rk = c.sb("colrow_stage", (32, 128), F32)
    col = c.sb(key, (128, 8), F32)
    id32_sb = c.sb("id32_sb", (128, 128), F32)
    b.dma("sp", row, vec.ap().rearrange("(c p) -> c p", p=128), writes=(rk,),
          tag="ldcol " + name)
    b.op("pe", lambda e: e.transpose(ps_bank[0:128, 0:8], row, id32_sb[0:8, 0:8]),
         reads=(rk, id32_sb), writes=(ps_bank,), tag="colT " + name)
    b.op("dve", lambda e: e.tensor_copy(col[:], ps_bank[0:128, 0:8]),
         reads=(ps_bank,), writes=(col,), tag="colC " + name)
    return col


def load_row_bc(c, name, dram_name, ps_bank, row_ap, row_key, out_ap, out_key):
    b = c.b
    vec = c.dram(dram_name, (D,), F32)
    ones_bc = c.sb("ones_bc", (1, 128), F32)
    b.dma("sp", row_ap, vec.ap().rearrange("(o f) -> o f", o=1), writes=(row_key,),
          tag="ldrow " + name)
    for g in range(2):
        b.op("pe", lambda e, g=g: e.matmul(
            ps_bank[:, 0:TT], ones_bc[:], row_ap[:, g * TT:(g + 1) * TT],
            start=True, stop=True),
            reads=(row_key, ones_bc), writes=(ps_bank,), tag="rbcM " + name)
        b.op("dve", lambda e, g=g: e.tensor_copy(out_ap[:, g * TT:(g + 1) * TT],
                                                 ps_bank[:, 0:TT]),
            reads=(ps_bank,), writes=(out_key,), tag="rbcC " + name)


def ph1(c):
    b = c.b
    b.phase("ph1")
    xnb = d_xnb(c)
    kv_in = d_kv_in(c)
    wk = c.dram("L0_attn_Wk", (D, D), F32)
    wv = c.dram("L0_attn_Wv", (D, D), F32)

    S = c.slot
    wk_sb = bf16v(S(5))
    wv_sb = bf16v(S(6))
    s2b = bf16v(S(2))
    xnt = [s2b[:, i * 1024:(i + 1) * 1024] for i in range(2)]
    ktm = s2b[:, 2048:3072]
    vaug = s2b[:, 3072:3072 + H * 65]
    tmpA = S(0)[:, 0:D]
    tmpB = S(0)[:, D:2 * D]
    kvsb = S(1)[0:64, 2048:2048 + H * 65]

    ps_k = [c.ps("ps0"), c.ps("ps1")]
    ps_v = [c.ps("ps2"), c.ps("ps3")]
    ps_kv = [c.ps("ps4"), c.ps("ps5"), c.ps("ps6")]
    ps_misc = c.ps("ps7")

    b.dma("pl", wk_sb.rearrange("p (c o) -> p c o", c=NCH),
          wk.ap().rearrange("(c p) o -> p c o", p=128), writes=(S(5),), tag="p1 wk")
    b.dma("pl", wv_sb.rearrange("p (c o) -> p c o", c=NCH),
          wv.ap().rearrange("(c p) o -> p c o", p=128), writes=(S(6),), tag="p1 wv")
    bk_bc = S(1)[:, 0:D]
    bv_bc = S(1)[:, D:2 * D]
    rowtmp = S(3)[0:1, 0:D]
    load_row_bc(c, "bk", "L0_attn_bk", ps_misc, rowtmp, S(3), bk_bc, S(1))
    load_row_bc(c, "bv", "L0_attn_bv", ps_misc, rowtmp, S(3), bv_bc, S(1))
    for hh in range(H):
        b.op("dve", lambda e, hh=hh: e.memset(vaug[:, hh * 65 + 64: hh * 65 + 65], 1.0),
             writes=(S(2),), tag="p1 vone")

    NT128 = TLOC // 128
    for ti in range(NT128):
        t0 = ti * 128
        xt = xnt[ti % 2]
        b.dma("sp", xt.rearrange("p (c t) -> p c t", c=NCH),
              fm_ap(xnb)[:, :, t0:t0 + 128], writes=(S(2),), tag="p1 ld%d" % ti)
        for which, wsb, psb in (("k", wk_sb, ps_k), ("v", wv_sb, ps_v)):
            for g in range(2):
                for ch in range(NCH):
                    b.op("pe", lambda e, g=g, s=ch, wsb=wsb, psb=psb, xt=xt: e.matmul(
                        psb[g][:, 0:TT],
                        xt[:, s * 128:(s + 1) * 128],
                        wsb[:, s * D + g * TT: s * D + (g + 1) * TT],
                        start=(s == 0), stop=(s == NCH - 1)),
                        reads=(S(2), S(5) if which == "k" else S(6)),
                        writes=(psb[g],), tag="p1 %sproj" % which)
        for g in range(2):
            sl = slice(g * TT, (g + 1) * TT)
            b.op("dve", lambda e, sl=sl, g=g: e.tensor_tensor(
                tmpA[:, sl], ps_k[g][:, 0:TT], bk_bc[:, sl], ALU.add),
                reads=(ps_k[g], S(1)), writes=(S(0),), tag="p1 kb")
            b.op("dve", lambda e, sl=sl: e.tensor_scalar_min(tmpB[:, sl], tmpA[:, sl], 0.0),
                 reads=(S(0),), writes=(S(0),), tag="p1 min")
            b.op("act", lambda e, sl=sl: e.activation(tmpB[:, sl], tmpB[:, sl], AF.Exp),
                 reads=(S(0),), writes=(S(0),), tag="p1 exp")
            b.op("dve", lambda e, sl=sl: e.tensor_scalar_max(tmpA[:, sl], tmpA[:, sl], 0.0),
                 reads=(S(0),), writes=(S(0),), tag="p1 max")
            b.op("dve", lambda e, sl=sl: e.tensor_tensor(ktm[:, sl], tmpA[:, sl],
                                                         tmpB[:, sl], ALU.add),
                 reads=(S(0),), writes=(S(2),), tag="p1 K")
        for hh in range(H):
            g = hh // 8
            b.op("dve", lambda e, hh=hh, g=g: e.tensor_tensor(
                vaug[:, hh * 65: hh * 65 + 64],
                ps_v[g][:, (hh % 8) * 64:(hh % 8) * 64 + 64],
                bv_bc[:, hh * 64:(hh + 1) * 64], ALU.add),
                reads=(ps_v[g], S(1)), writes=(S(2),), tag="p1 vh")
        for hh in range(H):
            bank = ps_kv[hh // 7] if hh < 14 else ps_kv[2]
            off = (hh % 7) * 65 if hh < 14 else (hh - 14) * 65
            b.op("pe", lambda e, hh=hh, bank=bank, off=off: e.matmul(
                bank[0:64, off:off + 65],
                ktm[:, hh * 64:(hh + 1) * 64],
                vaug[:, hh * 65:(hh + 1) * 65],
                start=(ti == 0), stop=(ti == NT128 - 1)),
                reads=(S(2),), writes=(bank,), tag="p1 kv%d" % hh)
    for k in range(3):
        cols = 455 if k < 2 else 130
        b.op("dve", lambda e, k=k, cols=cols: e.tensor_copy(
            kvsb[:, k * 455: k * 455 + cols], ps_kv[k][0:64, 0:cols]),
            reads=(ps_kv[k],), writes=(S(1),), tag="p1 kvev")
    b.dma("sp", kv_in.ap(), kvsb, reads=(S(1),), writes=(kv_in,), tag="p1 st_kv")



PAIRS = [[0, 1], [2, 3], [4, 5], [6, 7]]


def exchange(c, din, dout, tag):
    b = c.b
    b.cc(lambda e: e.collective_compute(
        "AllReduce", ALU.add, replica_groups=PAIRS,
        ins=[din.ap().opt()], outs=[dout.ap().opt()]),
        reads=(din,), writes=(dout,), tag=tag)


def load_reduced(c, base_in, base_out, base_pair, dest_ap, dest_key, tmp_ap, tmp_key, shape, tag):
    b = c.b
    if c.fused:
        b.dma("sp", dest_ap, base_out.ap(), reads=(base_out,), writes=(dest_key,), tag=tag)
    else:
        b.dma("sp", dest_ap, base_in.ap(), reads=(base_in,), writes=(dest_key,), tag=tag + "a")
        b.dma("sp", tmp_ap, base_pair.ap(), reads=(base_pair,), writes=(tmp_key,), tag=tag + "b")
        b.op("dve", lambda e: e.tensor_tensor(dest_ap, dest_ap, tmp_ap, ALU.add),
             reads=(dest_key, tmp_key), writes=(dest_key,), tag=tag + "add")


def phx1(c):
    b = c.b
    b.phase("phx1")
    if c.fused:
        exchange(c, d_kv_in(c), d_kv_out(c), "x1 kv")



def ph2(c):
    b = c.b
    b.phase("ph2")
    xnb = d_xnb(c); onehotq = d_onehot(c); wrowq = d_wrow(c); g0 = d_g0(c)
    w1 = c.dram("L0_ffn_W1", (NEXP, D, FF), F32)
    w2 = c.dram("L0_ffn_W2", (NEXP, FF, D), F32)
    selc = d_sel(c)

    S = c.slot
    HALF = 2048

    def y_view(ch, tsl=None):
        sl = S(ch // 2)
        base = (ch % 2) * HALF
        if tsl is None:
            return sl[:, base:base + HALF]
        return sl[:, base + tsl[0]: base + tsl[1]]

    def y_key(ch):
        return S(ch // 2)

    def xn_view(ch, a, bnd):
        slv = bf16v(S(4 + ch // 4))
        base = (ch % 4) * HALF
        return slv[:, base + a: base + bnd]

    w1_sb = bf16v(S(6))
    w2_sb = bf16v(S(7))
    h_sb = bf16v(S(8))
    s9b = bf16v(S(9))
    oh_half = s9b[0:8, 0:HALF]
    sel_sb = s9b[0:8, HALF:HALF + 1024]
    bc_sb = s9b[:, 3072:3584]
    w_t = f32v(S(9))[0:1, 2048:2048 + TT]
    ones_bc = c.sb("ones_bc", (1, 128), F32)

    ps_h = [c.ps("ps0"), c.ps("ps1")]
    ps_y = [c.ps("ps2"), c.ps("ps3")]
    ps_bc = c.ps("ps4")

    b.dma("pl", sel_sb, selc.ap(), writes=(S(9),), tag="p2 sel")

    for half in range(2):
        h0 = half * HALF
        for sl4 in range(2):
            b.dma("sp", bf16v(S(4 + sl4)).rearrange("p (c t) -> p c t", c=4),
                  fm_ap(xnb)[:, sl4 * 4:(sl4 + 1) * 4, h0:h0 + HALF],
                  writes=(S(4 + sl4),), tag="p2 xn%d" % sl4)
        b.dma("sp", oh_half, onehotq.ap()[:, h0:h0 + HALF], writes=(S(9),), tag="p2 oh")

        for e in range(NEXP):
            for q in range(4):
                b.dma("pl", w1_sb.rearrange("p (ci o) -> p ci o", ci=NCH),
                      w1.ap()[e, :, q * 1024:(q + 1) * 1024].rearrange(
                          "(ci p) o -> p ci o", p=128),
                      writes=(S(6),), tag="p2 w1 e%dq%d" % (e, q))
                b.dma("pl", w2_sb.rearrange("p (hc o) -> p hc o", hc=NCH),
                      w2.ap()[e, q * 1024:(q + 1) * 1024, :].rearrange(
                          "(hc p) o -> p hc o", p=128),
                      writes=(S(7),), tag="p2 w2 e%dq%d" % (e, q))
                for t5 in range(4):
                    ta, tb = t5 * TT, (t5 + 1) * TT
                    b.op("pe", lambda e_, e=e, ta=ta, tb=tb: e_.matmul(
                        ps_bc[:, 0:TT], sel_sb[:, 128 * e:128 * e + 128],
                        oh_half[:, ta:tb], start=True, stop=True),
                        reads=(S(9),), writes=(ps_bc,), tag="p2 bcm")
                    b.op("dve", lambda e_: e_.tensor_copy(bc_sb, ps_bc[:, 0:TT]),
                         reads=(ps_bc,), writes=(S(9),), tag="p2 bcc")
                    for hc in range(NCH):
                        pb = ps_h[hc % 2]
                        for ci in range(NCH):
                            b.op("pe", lambda e_, ci=ci, hc=hc, pb=pb, ta=ta, tb=tb: e_.matmul(
                                pb[:, 0:TT],
                                w1_sb[:, ci * 1024 + hc * 128: ci * 1024 + hc * 128 + 128],
                                xn_view(ci, ta, tb),
                                start=(ci == 0), stop=(ci == NCH - 1)),
                                reads=(S(6), S(4 + ci // 4)), writes=(pb,), tag="p2 h")
                        b.op("act", lambda e_, hc=hc, pb=pb: e_.activation(
                            h_sb[:, hc * TT:(hc + 1) * TT], pb[:, 0:TT], AF.Gelu),
                            reads=(pb,), writes=(S(8),), tag="p2 gelu")
                        b.op("dve", lambda e_, hc=hc: e_.tensor_tensor(
                            h_sb[:, hc * TT:(hc + 1) * TT],
                            h_sb[:, hc * TT:(hc + 1) * TT], bc_sb, ALU.mult),
                            reads=(S(8), S(9)), writes=(S(8),), tag="p2 hm")
                    for oc in range(NCH):
                        pb = ps_y[oc % 2]
                        for hc in range(NCH):
                            b.op("pe", lambda e_, hc=hc, oc=oc, pb=pb: e_.matmul(
                                pb[:, 0:TT],
                                w2_sb[:, hc * 1024 + oc * 128: hc * 1024 + oc * 128 + 128],
                                h_sb[:, hc * TT:(hc + 1) * TT],
                                start=(hc == 0), stop=(hc == NCH - 1)),
                                reads=(S(7), S(8)), writes=(pb,), tag="p2 y")
                        first = (e == 0 and q == 0)
                        if first:
                            b.op("dve", lambda e_, oc=oc, pb=pb, ta=ta, tb=tb: e_.tensor_copy(
                                y_view(oc, (ta, tb)), pb[:, 0:TT]),
                                reads=(pb,), writes=(y_key(oc),), tag="p2 yc")
                        else:
                            b.op("dve", lambda e_, oc=oc, pb=pb, ta=ta, tb=tb: e_.tensor_tensor(
                                y_view(oc, (ta, tb)), y_view(oc, (ta, tb)),
                                pb[:, 0:TT], ALU.add),
                                reads=(pb, y_key(oc)), writes=(y_key(oc),), tag="p2 ya")
        for t5 in range(4):
            ta, tb = t5 * TT, (t5 + 1) * TT
            b.dma("sp", w_t, wrowq.ap()[:, h0 + ta:h0 + tb], writes=(S(9),), tag="p2 wl")
            b.op("pe", lambda e_: e_.matmul(
                ps_bc[:, 0:TT], ones_bc[:], w_t, start=True, stop=True),
                reads=(ones_bc, S(9)), writes=(ps_bc,), tag="p2 wbc")
            for ch in range(NCH):
                b.op("dve", lambda e_, ch=ch, ta=ta, tb=tb: e_.tensor_tensor(
                    y_view(ch, (ta, tb)), y_view(ch, (ta, tb)), ps_bc[:, 0:TT], ALU.mult),
                    reads=(ps_bc, y_key(ch)), writes=(y_key(ch),), tag="p2 yw")
                b.op("act", lambda e_, ch=ch, ta=ta, tb=tb: e_.activation(
                    y_view(ch, (ta, tb)), y_view(ch, (ta, tb)), AF.Silu),
                    reads=(y_key(ch),), writes=(y_key(ch),), tag="p2 silu")
        for ch in range(NCH):
            b.dma("sp", fm_ap(g0)[:, ch:ch + 1, h0:h0 + HALF],
                  y_view(ch).rearrange("p (o t) -> p o t", o=1),
                  reads=(y_key(ch),), writes=(g0,), tag="p2 stg")



def load_colN(c, name, dram_name, nch, ps_bank):
    b = c.b
    key = "col_" + name
    if key in c.sb_t:
        return c.sb_t[key]
    vec = c.dram(dram_name, (nch * 128,), F32)
    row = c.sb("colrow_stage", (32, 128), F32)[0:nch, :]
    rk = c.sb("colrow_stage", (32, 128), F32)
    col = c.sb(key, (128, nch), F32)
    id32_sb = c.sb("id32_sb", (128, 128), F32)
    b.dma("sp", row, vec.ap().rearrange("(c p) -> c p", p=128), writes=(rk,),
          tag="ldcol " + name)
    b.op("pe", lambda e: e.transpose(ps_bank[0:128, 0:nch], row, id32_sb[0:nch, 0:nch]),
         reads=(rk, id32_sb), writes=(ps_bank,), tag="colT " + name)
    b.op("dve", lambda e: e.tensor_copy(col[:], ps_bank[0:128, 0:nch]),
         reads=(ps_bank,), writes=(col,), tag="colC " + name)
    return col


def ph3(c):
    b = c.b
    b.phase("ph3")
    x0fm = d_x0fm(c); g0 = d_g0(c); x1fm = d_x1fm(c); xn1b = d_xn1b(c)
    wq = c.dram("L0_attn_Wq", (D, D), F32)
    wo = c.dram("L0_attn_Wo", (D, D), F32)
    w1 = c.dram("L0_attn_W1", (D, FF), F32)
    w2 = c.dram("L0_attn_W2", (FF, D), F32)
    repselc = d_repsel(c)
    halo_in = d_halo_in(c)

    S = c.slot
    HALF = 2048

    ps0 = c.ps("ps0"); ps1 = c.ps("ps1")
    ps_s1 = c.ps("ps2"); ps_s2 = c.ps("ps3")
    ps_r = c.ps("ps4"); ps_nmr = c.ps("ps5")
    ps_z = c.ps("ps6"); ps_x = c.ps("ps7")

    bq_col = load_colN(c, "bq", "L0_attn_bq", 8, ps_x)
    bo_col = load_colN(c, "bo", "L0_attn_bo", 8, ps_x)
    b1_col = load_colN(c, "b1", "L0_attn_b1", 32, ps_x)
    b2_col = load_colN(c, "b2", "L0_attn_b2", 8, ps_x)
    g1_col = load_colN(c, "g1", "L0_attn_g1", 8, ps_x)
    be1_col = load_colN(c, "be1", "L0_attn_be1", 8, ps_x)
    g2_col = load_colN(c, "g2", "L0_attn_g2", 8, ps_x)
    be2_col = load_colN(c, "be2", "L0_attn_be2", 8, ps_x)
    row_r = c.sb("row_r", (1, TT), F32)
    row_nmr = c.sb("row_nmr", (1, TT), F32)
    zrow = S(0)[0:16, 3104:3104 + TT]
    repsel_sb = S(0)[0:16, 2080:2080 + 1024]
    kvt = S(0)[0:64, 0:H * 65]
    kvtmp = S(0)[0:64, H * 65:2 * H * 65]
    kvb = c.sb("p3_kvb", (128, H * DH), BF16)
    ksumcol = c.sb("p3_ksum", (128, 8), F32)
    halo_sb = c.sb("halo_sb", (128, 8), F32)
    nz_col = c.sb("nz_col", (128, 1), F32)
    zmask_sb = c.sb("zmask_sb", (128, 1), F32)
    ones_bc = c.sb("ones_bc", (1, 128), F32)

    zc = d_zmask(c)
    b.dma("sp", zmask_sb[:], zc.ap(), writes=(zmask_sb,), tag="p3 zm")
    b.op("dve", lambda e: e.tensor_scalar(nz_col[:], zmask_sb[:], -1.0, 1.0,
                                          ALU.mult, ALU.add),
         reads=(zmask_sb,), writes=(nz_col,), tag="p3 nz")

    load_reduced(c, d_kv_in(c), d_kv_out(c), d_kv_pair(c), kvt, S(0), kvtmp, S(0),
                 (64, H * 65), "p3 kvld")
    kvt3 = kvt.rearrange("s (h m) -> s h m", m=65)[:, :, 0:64]
    for par in range(2):
        b.dma("pl", kvb[par * 64:(par + 1) * 64, :].rearrange("s (h m) -> s h m", m=64),
              kvt3, reads=(S(0),), writes=(kvb,), tag="p3 kvb%d" % par)
    kvt4 = kvt.rearrange("s (ch par m) -> s ch par m", par=2, m=65)
    for par in range(2):
        b.dma("pl", ksumcol[par * 64:(par + 1) * 64, :].rearrange("s (ch o) -> s ch o", o=1),
              kvt4[:, :, par, 64:65], reads=(S(0),), writes=(ksumcol,),
              tag="p3 ks%d" % par, allow_slow_non_contiguous=True)

    wq_sb = bf16v(S(9))
    wo_sb = bf16v(S(10))
    xg = bf16v(S(4))

    def xmid_v(ch, a, bnd):
        return S(5 + ch // 2)[:, (ch % 2) * HALF + a:(ch % 2) * HALF + bnd]

    def y_v(ch, a, bnd):
        return S(ch // 2)[:, (ch % 2) * HALF + a:(ch % 2) * HALF + bnd]

    for hf in range(2):
        h0 = hf * HALF
        b.dma("sp", repsel_sb, repselc.ap(), writes=(S(0),), tag="p3 repsel%d" % hf)
        b.dma("pl", wq_sb.rearrange("p (ci o) -> p ci o", ci=NCH),
              wq.ap().rearrange("(ci p) o -> p ci o", p=128), writes=(S(9),),
              tag="p3 wq%d" % hf)
        b.dma("pl", wo_sb.rearrange("p (ci o) -> p ci o", ci=NCH),
              wo.ap().rearrange("(ci p) o -> p ci o", p=128), writes=(S(10),),
              tag="p3 wo%d" % hf)
        for ti in range(4):
            t0 = h0 + ti * TT
            xv = f32v(S(11))
            b.dma("sp", xv.rearrange("p (ch t) -> p ch t", ch=NCH),
                  fm_ap(x0fm)[:, :, t0:t0 + TT], writes=(S(11),), tag="p3 ldx")
            ln_rows(c, "p3 ln0 ", xv, f32v(S(3)), ps_s1, ps_s2, row_r, row_nmr, EPS_LN)
            bcast_row(c, "p3 bcr0", row_r, ps_r)
            bcast_row(c, "p3 bcn0", row_nmr, ps_nmr)
            apply_ln(c, "p3 xn ", xv, ps_r, ps_nmr, f32v(S(1)))
            for ch in range(NCH):
                b.op("dve", lambda e, ch=ch: e.tensor_copy(
                    xg[:, ch * TT:(ch + 1) * TT], f32v(S(1))[:, ch * TT:(ch + 1) * TT]),
                    reads=(S(1),), writes=(S(4),), tag="p3 xnb")
            for mc in range(NCH):
                pb = ps0 if mc % 2 == 0 else ps1
                for ci in range(NCH):
                    b.op("pe", lambda e, ci=ci, mc=mc, pb=pb: e.matmul(
                        pb[:, 0:TT],
                        wq_sb[:, ci * D + mc * 128: ci * D + mc * 128 + 128],
                        xg[:, ci * TT:(ci + 1) * TT],
                        start=(ci == 0), stop=(ci == NCH - 1)),
                        reads=(S(9), S(4)), writes=(pb,), tag="p3 q")
                sl = slice(mc * TT, (mc + 1) * TT)
                b.op("dve", lambda e, mc=mc, pb=pb, sl=sl: e.tensor_scalar(
                    f32v(S(2))[:, sl], pb[:, 0:TT], bq_col[:, mc:mc + 1], None, ALU.add),
                    reads=(pb, bq_col), writes=(S(2),), tag="p3 qb")
                b.op("dve", lambda e, sl=sl: e.tensor_scalar_min(
                    f32v(S(3))[:, sl], f32v(S(2))[:, sl], 0.0),
                    reads=(S(2),), writes=(S(3),), tag="p3 qmin")
                b.op("act", lambda e, sl=sl: e.activation(
                    f32v(S(3))[:, sl], f32v(S(3))[:, sl], AF.Exp),
                    reads=(S(3),), writes=(S(3),), tag="p3 qexp")
                b.op("dve", lambda e, sl=sl: e.tensor_scalar_max(
                    f32v(S(2))[:, sl], f32v(S(2))[:, sl], 0.0),
                    reads=(S(2),), writes=(S(2),), tag="p3 qmax")
                b.op("dve", lambda e, sl=sl: e.tensor_tensor(
                    f32v(S(2))[:, sl], f32v(S(2))[:, sl], f32v(S(3))[:, sl], ALU.add),
                    reads=(S(2), S(3)), writes=(S(2),), tag="p3 Q")
                b.op("dve", lambda e, sl=sl: e.tensor_copy(
                    xg[:, 4096 + sl.start: 4096 + sl.stop], f32v(S(2))[:, sl]),
                    reads=(S(2),), writes=(S(4),), tag="p3 Qb")
            for ch in range(NCH):
                sl = slice(ch * TT, (ch + 1) * TT)
                b.op("dve", lambda e, ch=ch, sl=sl: e.tensor_scalar(
                    f32v(S(3))[:, sl], f32v(S(2))[:, sl], ksumcol[:, ch:ch + 1],
                    None, ALU.mult),
                    reads=(S(2), ksumcol), writes=(S(3),), tag="p3 qk")
            blk1_sb = c.sb("blk1_sb", (128, 128), F32)
            if not getattr(c, "_blk1_loaded", False):
                c._blk1_loaded = True
                b.dma("sp", blk1_sb[:], d_blk1(c).ap(), writes=(blk1_sb,), tag="p3 blk1")
            for ch in range(NCH):
                b.op("pe", lambda e, ch=ch: e.matmul(
                    ps_z[0:16, 0:TT], blk1_sb[:, ch * 16:(ch + 1) * 16],
                    f32v(S(3))[:, ch * TT:(ch + 1) * TT],
                    start=(ch == 0), stop=(ch == NCH - 1)),
                    reads=(S(3), blk1_sb), writes=(ps_z,), tag="p3 zden")
            b.op("dve", lambda e: e.tensor_scalar(zrow, ps_z[0:16, 0:TT], EPS_Z,
                                                  None, ALU.add),
                 reads=(ps_z,), writes=(S(0),), tag="p3 zeps")
            b.op("dve", lambda e: e.reciprocal(zrow, zrow),
                 reads=(S(0),), writes=(S(0),), tag="p3 zrec")
            for ch in range(NCH):
                pb = ps0 if ch % 2 == 0 else ps1
                b.op("pe", lambda e, ch=ch: e.matmul(
                    ps_x[:, 0:TT], repsel_sb[:, ch * 128:(ch + 1) * 128], zrow,
                    start=True, stop=True),
                    reads=(S(0),), writes=(ps_x,), tag="p3 zbc")
                b.op("dve", lambda e, ch=ch: e.tensor_copy(
                    f32v(S(3))[:, ch * TT:(ch + 1) * TT], ps_x[:, 0:TT]),
                    reads=(ps_x,), writes=(S(3),), tag="p3 zbcc")
                for par in range(2):
                    hh = 2 * ch + par
                    b.op("pe", lambda e, ch=ch, par=par, hh=hh, pb=pb: e.matmul(
                        pb[par * 64:(par + 1) * 64, 0:TT],
                        kvb[par * 64:(par + 1) * 64, hh * DH:(hh + 1) * DH],
                        xg[par * 64:(par + 1) * 64, 4096 + ch * TT: 4096 + (ch + 1) * TT],
                        start=True, stop=True),
                        reads=(kvb, S(4)), writes=(pb,), tag="p3 o")
                b.op("dve", lambda e, ch=ch, pb=pb: e.tensor_tensor(
                    xg[:, ch * TT:(ch + 1) * TT], pb[:, 0:TT],
                    f32v(S(3))[:, ch * TT:(ch + 1) * TT], ALU.mult),
                    reads=(pb, S(3)), writes=(S(4),), tag="p3 oz")
            for mc in range(NCH):
                pb = ps0 if mc % 2 == 0 else ps1
                for ci in range(NCH):
                    b.op("pe", lambda e, ci=ci, mc=mc, pb=pb: e.matmul(
                        pb[:, 0:TT],
                        wo_sb[:, ci * D + mc * 128: ci * D + mc * 128 + 128],
                        xg[:, ci * TT:(ci + 1) * TT],
                        start=(ci == 0), stop=(ci == NCH - 1)),
                        reads=(S(10), S(4)), writes=(pb,), tag="p3 ao")
                b.op("dve", lambda e, mc=mc, pb=pb: e.scalar_tensor_tensor(
                    f32v(S(2))[:, mc * TT:(mc + 1) * TT], pb[:, 0:TT],
                    bo_col[:, mc:mc + 1], f32v(S(1))[:, mc * TT:(mc + 1) * TT],
                    ALU.add, ALU.add),
                    reads=(pb, bo_col, S(1)), writes=(S(2),), tag="p3 r0")
            ln_rows(c, "p3 ln1 ", f32v(S(2)), f32v(S(3)), ps_s1, ps_s2,
                    row_r, row_nmr, EPS_LN)
            bcast_row(c, "p3 bcr1", row_r, ps_r)
            bcast_row(c, "p3 bcn1", row_nmr, ps_nmr)
            for ch in range(NCH):
                sl = slice(ch * TT, (ch + 1) * TT)
                b.op("dve", lambda e, sl=sl: e.tensor_tensor(
                    f32v(S(3))[:, sl], f32v(S(2))[:, sl], ps_r[:, 0:TT], ALU.mult),
                    reads=(S(2), ps_r), writes=(S(3),), tag="p3 xm1")
                b.op("dve", lambda e, sl=sl: e.tensor_tensor(
                    f32v(S(3))[:, sl], f32v(S(3))[:, sl], ps_nmr[:, 0:TT], ALU.add),
                    reads=(S(3), ps_nmr), writes=(S(3),), tag="p3 xm2")
                b.op("dve", lambda e, ch=ch, ti=ti, sl=sl: e.tensor_scalar(
                    xmid_v(ch, ti * TT, (ti + 1) * TT), f32v(S(3))[:, sl],
                    g1_col[:, ch:ch + 1], be1_col[:, ch:ch + 1], ALU.mult, ALU.add),
                    reads=(S(3), g1_col, be1_col), writes=(S(5 + ch // 2),), tag="p3 xm3")
        for q in range(4):
            b.dma("pl", bf16v(S(9)).rearrange("p (ci o) -> p ci o", ci=NCH),
                  w1.ap()[:, q * 1024:(q + 1) * 1024].rearrange("(ci p) o -> p ci o", p=128),
                  writes=(S(9),), tag="p3 w1q")
            b.dma("pl", bf16v(S(10)).rearrange("p (hc o) -> p hc o", hc=NCH),
                  w2.ap()[q * 1024:(q + 1) * 1024, :].rearrange("(hc p) o -> p hc o", p=128),
                  writes=(S(10),), tag="p3 w2q")
            for t5 in range(4):
                ta, tb = t5 * TT, (t5 + 1) * TT
                for ch in range(NCH):
                    b.op("dve", lambda e, ch=ch, ta=ta, tb=tb: e.tensor_copy(
                        xg[:, ch * TT:(ch + 1) * TT], xmid_v(ch, ta, tb)),
                        reads=(S(5 + ch // 2),), writes=(S(4),), tag="p3 xmb")
                for hc in range(NCH):
                    pb = ps0 if hc % 2 == 0 else ps1
                    for ci in range(NCH):
                        b.op("pe", lambda e, ci=ci, hc=hc, pb=pb: e.matmul(
                            pb[:, 0:TT],
                            bf16v(S(9))[:, ci * 1024 + hc * 128: ci * 1024 + hc * 128 + 128],
                            xg[:, ci * TT:(ci + 1) * TT],
                            start=(ci == 0), stop=(ci == NCH - 1)),
                            reads=(S(9), S(4)), writes=(pb,), tag="p3 h")
                    b.op("act", lambda e, hc=hc, q=q, pb=pb: e.activation(
                        xg[:, 4096 + hc * TT: 4096 + (hc + 1) * TT], pb[:, 0:TT],
                        AF.Gelu, bias=b1_col[:, q * 8 + hc: q * 8 + hc + 1]),
                        reads=(pb, b1_col), writes=(S(4),), tag="p3 gelu")
                for oc in range(NCH):
                    pb = ps_s1 if oc % 2 == 0 else ps_s2
                    for hc in range(NCH):
                        b.op("pe", lambda e, hc=hc, oc=oc, pb=pb: e.matmul(
                            pb[:, 0:TT],
                            bf16v(S(10))[:, hc * 1024 + oc * 128: hc * 1024 + oc * 128 + 128],
                            xg[:, 4096 + hc * TT: 4096 + (hc + 1) * TT],
                            start=(hc == 0), stop=(hc == NCH - 1)),
                            reads=(S(10), S(4)), writes=(pb,), tag="p3 y")
                    if q == 0:
                        b.op("dve", lambda e, oc=oc, pb=pb, ta=ta, tb=tb: e.tensor_copy(
                            y_v(oc, ta, tb), pb[:, 0:TT]),
                            reads=(pb,), writes=(S(oc // 2),), tag="p3 yc")
                    else:
                        b.op("dve", lambda e, oc=oc, pb=pb, ta=ta, tb=tb: e.tensor_tensor(
                            y_v(oc, ta, tb), y_v(oc, ta, tb), pb[:, 0:TT], ALU.add),
                            reads=(pb, S(oc // 2)), writes=(S(oc // 2),), tag="p3 ya")
        for ti in range(4):
            t0 = h0 + ti * TT
            ta, tb = ti * TT, (ti + 1) * TT
            r1 = f32v(S(9))
            tmp = f32v(S(10))
            for ch in range(NCH):
                b.op("dve", lambda e, ch=ch, ta=ta, tb=tb: e.scalar_tensor_tensor(
                    r1[:, ch * TT:(ch + 1) * TT], y_v(ch, ta, tb), b2_col[:, ch:ch + 1],
                    xmid_v(ch, ta, tb), ALU.add, ALU.add),
                    reads=(S(ch // 2), b2_col, S(5 + ch // 2)), writes=(S(9),), tag="p3 r1")
            ln_rows(c, "p3 ln2 ", r1, tmp, ps_s1, ps_s2, row_r, row_nmr, EPS_LN)
            bcast_row(c, "p3 bcr2", row_r, ps_r)
            bcast_row(c, "p3 bcn2", row_nmr, ps_nmr)
            apply_ln(c, "p3 s ", r1, ps_r, ps_nmr, r1)
            for ch in range(NCH):
                sl = slice(ch * TT, (ch + 1) * TT)
                b.op("dve", lambda e, sl=sl, ch=ch: e.tensor_scalar(
                    r1[:, sl], r1[:, sl], g2_col[:, ch:ch + 1], be2_col[:, ch:ch + 1],
                    ALU.mult, ALU.add),
                    reads=(S(9), g2_col, be2_col), writes=(S(9),), tag="p3 sgb")
            gv = f32v(S(4))
            xv = f32v(S(11))
            b.dma("sp", gv.rearrange("p (ch t) -> p ch t", ch=NCH),
                  fm_ap(g0)[:, :, t0:t0 + TT], writes=(S(4),), tag="p3 ldg")
            b.dma("sp", xv.rearrange("p (ch t) -> p ch t", ch=NCH),
                  fm_ap(x0fm)[:, :, t0:t0 + TT], writes=(S(11),), tag="p3 ldx2")
            for ch in range(NCH):
                sl = slice(ch * TT, (ch + 1) * TT)
                b.op("dve", lambda e, sl=sl: e.tensor_tensor(
                    r1[:, sl], r1[:, sl], gv[:, sl], ALU.mult),
                    reads=(S(9), S(4)), writes=(S(9),), tag="p3 sg")
                b.op("dve", lambda e, sl=sl: e.tensor_tensor(
                    xv[:, sl], xv[:, sl], r1[:, sl], ALU.add),
                    reads=(S(11), S(9)), writes=(S(11),), tag="p3 x1")
            b.dma("sp", fm_ap(x1fm)[:, :, t0:t0 + TT],
                  xv.rearrange("p (ch t) -> p ch t", ch=NCH),
                  reads=(S(11),), writes=(x1fm,), tag="p3 stx1")
            ln_rows(c, "p3 ln3 ", xv, tmp, ps_s1, ps_s2, row_r, row_nmr, EPS_LN)
            bcast_row(c, "p3 bcr3", row_r, ps_r)
            bcast_row(c, "p3 bcn3", row_nmr, ps_nmr)
            apply_ln(c, "p3 xn1 ", xv, ps_r, ps_nmr, tmp)
            for ch in range(NCH):
                b.op("dve", lambda e, ch=ch: e.tensor_copy(
                    xg[:, ch * TT:(ch + 1) * TT], tmp[:, ch * TT:(ch + 1) * TT]),
                    reads=(S(10),), writes=(S(4),), tag="p3 xn1b")
            b.dma("sp", fm_ap(xn1b)[:, :, t0:t0 + TT],
                  xg[:, 0:4096].rearrange("p (ch t) -> p ch t", ch=NCH),
                  reads=(S(4),), writes=(xn1b,), tag="p3 stxn1")
            if hf == 1 and ti == 3:
                for ch in range(NCH):
                    b.op("dve", lambda e, ch=ch: e.tensor_copy(
                        halo_sb[:, ch:ch + 1], tmp[:, ch * TT + TT - 1: ch * TT + TT]),
                        reads=(S(10),), writes=(halo_sb,), tag="p3 halo")
                b.op("dve", lambda e: e.tensor_tensor(
                    halo_sb[:], halo_sb[:], nz_col[:, 0:1].broadcast_to([128, 8]), ALU.mult),
                    reads=(halo_sb, nz_col), writes=(halo_sb,), tag="p3 halom")
                b.dma("sp", halo_in.ap(), halo_sb[:], reads=(halo_sb,),
                      writes=(halo_in,), tag="p3 sthalo")



def ph4(c):
    b = c.b
    b.phase("ph4")
    xn1b = d_xn1b(c); yloc = d_yloc(c); qtb = d_qtb(c); st_in = d_st_in(c)
    wqkvT = c.dram("w_qkvT", (D, 3 * D), F32)
    Aq = d_ropeAq(c); Bq = d_ropeBq(c); Ak = d_ropeAk(c); Bk = d_ropeBk(c)
    p2c = d_p2(c); cauc = d_causal(c)

    S = c.slot
    xnt = bf16v(S(4))[:, 0:4096]
    qkf = f32v(S(5))
    qtil = bf16v(S(7))[:, 0:4096]
    ktil = bf16v(S(7))[:, 4096:8192]
    ytile = f32v(S(9))
    tabA = f32v(S(0)); tabB = f32v(S(1))
    tabAk = f32v(S(2)); tabBk = f32v(S(3))
    tmp1 = f32v(S(8))[:, 0:TT]
    tmp2 = f32v(S(8))[:, TT:2 * TT]
    v_tm = bf16v(S(8))[:, 2048:3072]
    ktm = bf16v(S(8))[:, 3072:4096]
    stm = bf16v(S(8))[:, 4224:4352]
    wq_sb = bf16v(S(10))
    wk_sb = bf16v(S(11))
    wv_sb = bf16v(S(6))
    p2_sb = f32v(S(8))[:, 2944:3072]
    cau_sb = f32v(S(8))[:, 3072:3200]
    sstate = c.sb("p4_state", (128, H * DH), F32)
    sb16 = bf16v(S(8))[:, 4352:4352 + H * DH]
    nz_col = c.sb("nz_col", (128, 1), F32)

    psA = c.ps("ps0"); psB = c.ps("ps1")
    ps_misc = c.ps("ps2")
    psS = c.ps("ps3")
    psY = [c.ps("ps4"), c.ps("ps5")]
    psD = [c.ps("ps6"), c.ps("ps7")]

    idb = idb_sb(c)
    b.dma("sp", p2_sb, p2c.ap(), writes=(S(8),), tag="p4 p2")
    b.dma("sp", cau_sb, cauc.ap(), writes=(S(8),), tag="p4 cau")
    b.op("dve", lambda e: e.memset(sstate[:], 0.0), writes=(sstate,), tag="p4 s0")
    for wname, wsb, off, wkey in (("q", wq_sb, 0, S(10)), ("k", wk_sb, D, S(11)),
                                  ("v", wv_sb, 2 * D, S(6))):
        b.dma("pl", wsb.rearrange("p (ci o) -> p ci o", ci=NCH),
              wqkvT.ap()[:, off:off + D].rearrange("(ci p) o -> p ci o", p=128),
              writes=(wkey,), tag="p4 w" + wname)

    GPOW = [[g ** 127, g ** 128] for g in GAMMA]

    for ti in range(NTT):
        t0 = ti * TT
        b.dma("sp", xnt.rearrange("p (ch t) -> p ch t", ch=NCH),
              fm_ap(xn1b)[:, :, t0:t0 + TT], writes=(S(4),), tag="p4 ldxn")
        for tab, dram in ((tabA, Aq), (tabB, Bq), (tabAk, Ak), (tabBk, Bk)):
            b.dma("sp", tab.rearrange("p (ch t) -> p ch t", ch=NCH),
                  fm_ap(dram)[:, :, t0:t0 + TT],
                  writes=(S([tabA, tabB, tabAk, tabBk].index(tab)),), tag="p4 ldtab")
        for which in ("q", "k"):
            wsb, wkey = (wq_sb, S(10)) if which == "q" else (wk_sb, S(11))
            tA, tB = (tabA, tabB) if which == "q" else (tabAk, tabBk)
            tAk, tBk = (S(0), S(1)) if which == "q" else (S(2), S(3))
            dst = qtil if which == "q" else ktil
            dstk = S(7)
            for mc in range(NCH):
                pb = psA if mc % 2 == 0 else psB
                for ci in range(NCH):
                    b.op("pe", lambda e, ci=ci, mc=mc, pb=pb, wsb=wsb: e.matmul(
                        pb[:, 0:TT],
                        wsb[:, ci * D + mc * 128: ci * D + mc * 128 + 128],
                        xnt[:, ci * TT:(ci + 1) * TT],
                        start=(ci == 0), stop=(ci == NCH - 1)),
                        reads=(wkey, S(4)), writes=(pb,), tag="p4 %sproj" % which)
                b.op("dve", lambda e, mc=mc, pb=pb: e.tensor_copy(
                    qkf[:, mc * TT:(mc + 1) * TT], pb[:, 0:TT]),
                    reads=(pb,), writes=(S(5),), tag="p4 %sev" % which)
            for ch in range(NCH):
                sl = slice(ch * TT, (ch + 1) * TT)
                b.op("pe", lambda e, sl=sl: e.matmul(
                    ps_misc[:, 0:TT], p2_sb, qkf[:, sl], start=True, stop=True),
                    reads=(S(8), S(5)), writes=(ps_misc,), tag="p4 rot")
                b.op("dve", lambda e, sl=sl, tA=tA: e.tensor_tensor(
                    tmp1[:], qkf[:, sl], tA[:, sl], ALU.mult),
                    reads=(S(5), tAk), writes=(S(8),), tag="p4 r1")
                b.op("dve", lambda e, sl=sl, tB=tB: e.tensor_tensor(
                    tmp2[:], ps_misc[:, 0:TT], tB[:, sl], ALU.mult),
                    reads=(ps_misc, tBk), writes=(S(8),), tag="p4 r2")
                b.op("dve", lambda e, sl=sl, dst=dst: e.tensor_tensor(
                    dst[:, sl], tmp1[:], tmp2[:], ALU.add),
                    reads=(S(8),), writes=(dstk,), tag="p4 r3")
        b.dma("sp", fm_ap(qtb)[:, :, t0:t0 + TT],
              qtil.rearrange("p (ch t) -> p ch t", ch=NCH),
              reads=(S(7),), writes=(qtb,), tag="p4 stq")
        for cc in range(4):
            for g in range(2):
                pb = psA if g == 0 else psB
                for ci in range(NCH):
                    b.op("pe", lambda e, g=g, ci=ci, pb=pb, cc=cc: e.matmul(
                        pb[:, 0:TT],
                        xnt[:, ci * TT + cc * 128: ci * TT + cc * 128 + 128],
                        wv_sb[:, ci * D + g * TT: ci * D + (g + 1) * TT],
                        start=(ci == 0), stop=(ci == NCH - 1)),
                        reads=(S(4), S(6)), writes=(pb,), tag="p4 vproj")
                b.op("dve", lambda e, g=g, pb=pb: e.tensor_copy(
                    v_tm[:, g * TT:(g + 1) * TT], pb[:, 0:TT]),
                    reads=(pb,), writes=(S(8),), tag="p4 vev")
            for ch in range(NCH):
                b.op("pe", lambda e, ch=ch, cc=cc: e.transpose(
                    ps_misc.ap().bitcast(BF16)[:, 0:128],
                    ktil[:, ch * TT + cc * 128: ch * TT + cc * 128 + 128], idb[:]),
                    reads=(S(7), idb), writes=(ps_misc,), tag="p4 kT")
                b.op("dve", lambda e, ch=ch: e.tensor_copy(
                    ktm[:, ch * 128:(ch + 1) * 128], ps_misc.ap().bitcast(BF16)[:, 0:128]),
                    reads=(ps_misc,), writes=(S(8),), tag="p4 kTc")
            b.dma("pl", sstate[64:128, :], sstate[0:64, :],
                  reads=(sstate,), writes=(sstate,), tag="p4 srep")
            for hh in range(H):
                b.op("dve", lambda e, hh=hh: e.tensor_scalar(
                    sb16[:, hh * DH:(hh + 1) * DH], sstate[:, hh * DH:(hh + 1) * DH],
                    float(GAMMA[hh]), None, ALU.mult),
                    reads=(sstate,), writes=(S(8),), tag="p4 sb")
            for hh in range(H):
                par, fch = hh % 2, hh // 2
                qsl = (slice(par * 64, par * 64 + 64),
                       slice(fch * TT + cc * 128, fch * TT + cc * 128 + 128))
                b.op("pe", lambda e, qsl=qsl: e.matmul(
                    psS[:, 0:128], ktil[qsl[0], qsl[1]], qtil[qsl[0], qsl[1]],
                    start=True, stop=True),
                    reads=(S(7),), writes=(psS,), tag="p4 st")
                b.op("dve", lambda e: e.tensor_tensor(
                    stm[:, 0:128], psS[:, 0:128], cau_sb, ALU.mult),
                    reads=(psS, S(8)), writes=(S(8),), tag="p4 mask")
                yb = psY[fch // 4]
                ysl = (slice(par * 64, par * 64 + 64),
                       slice((fch % 4) * 128, (fch % 4) * 128 + 128))
                b.op("pe", lambda e, hh=hh, yb=yb, ysl=ysl: e.matmul(
                    yb[ysl[0], ysl[1]], v_tm[:, hh * DH:(hh + 1) * DH], stm[:, 0:128],
                    start=True, stop=False),
                    reads=(S(8),), writes=(yb,), tag="p4 yintra")
                b.op("pe", lambda e, hh=hh, yb=yb, ysl=ysl, qsl=qsl: e.matmul(
                    yb[ysl[0], ysl[1]], sb16[qsl[0], hh * DH:(hh + 1) * DH],
                    qtil[qsl[0], qsl[1]],
                    start=False, stop=True),
                    reads=(S(8), S(7)), writes=(yb,), tag="p4 yinter")
                db = psD[hh // 8]
                dsl = slice((hh % 8) * 64, (hh % 8) * 64 + 64)
                b.op("pe", lambda e, hh=hh, db=db, dsl=dsl: e.matmul(
                    db[0:64, dsl], ktm[:, hh * DH:(hh + 1) * DH],
                    v_tm[:, hh * DH:(hh + 1) * DH], start=True, stop=True),
                    reads=(S(8),), writes=(db,), tag="p4 delta")
                b.op("dve", lambda e, hh=hh: e.tensor_scalar(
                    sstate[0:64, hh * DH:(hh + 1) * DH],
                    sstate[0:64, hh * DH:(hh + 1) * DH],
                    float(GPOW[hh][1]), None, ALU.mult),
                    reads=(sstate,), writes=(sstate,), tag="p4 sdec")
                b.op("dve", lambda e, hh=hh, db=db, dsl=dsl: e.scalar_tensor_tensor(
                    sstate[0:64, hh * DH:(hh + 1) * DH], db[0:64, dsl],
                    float(GPOW[hh][0]), sstate[0:64, hh * DH:(hh + 1) * DH],
                    ALU.mult, ALU.add),
                    reads=(db, sstate), writes=(sstate,), tag="p4 supd")
            for fch in range(NCH):
                yb = psY[fch // 4]
                b.op("dve", lambda e, fch=fch, cc=cc, yb=yb: e.tensor_copy(
                    ytile[:, fch * TT + cc * 128: fch * TT + cc * 128 + 128],
                    yb[:, (fch % 4) * 128:(fch % 4) * 128 + 128]),
                    reads=(yb,), writes=(S(9),), tag="p4 yev")
        b.dma("sp", fm_ap(yloc)[:, :, t0:t0 + TT],
              ytile.rearrange("p (ch t) -> p ch t", ch=NCH),
              reads=(S(9),), writes=(yloc,), tag="p4 sty")
    b.op("dve", lambda e: e.tensor_tensor(
        sstate[0:64, :], sstate[0:64, :],
        nz_col[0:64, 0:1].broadcast_to([64, H * DH]), ALU.mult),
        reads=(sstate, nz_col), writes=(sstate,), tag="p4 smask")
    b.dma("sp", st_in.ap(), sstate[0:64, :], reads=(sstate,), writes=(st_in,),
          tag="p4 stst")


def idb_sb(c):
    if "idb_sb" not in c.sb_t:
        h = c.sb("idb_sb", (128, 128), BF16)
        c.b.dma("sp", h[:], d_idb(c).ap(), writes=(h,), tag="idb")
    return c.sb_t["idb_sb"]


def phx2(c):
    b = c.b
    b.phase("phx2")
    if c.fused:
        exchange(c, d_st_in(c), d_st_out(c), "x2 st")
        exchange(c, d_halo_in(c), d_halo_out(c), "x2 halo")
        if "dbg_st" in c.ext_out:
            dbg = c.dram("dbg_st", (64, H * 64), F32)
            b.dma("sp", dbg.ap(), d_st_out(c).ap(), reads=(d_st_out(c),),
                  writes=(dbg,), tag="x2 dbg")



def ph5(c):
    b = c.b
    b.phase("ph5")
    yloc = d_yloc(c); qtb = d_qtb(c); yfin = d_yfin(c); gn_in = d_gn_in(c)
    gcorr = c.dram("c_gcorr", (128, NCH * 32), F32)

    S = c.slot
    ytile = f32v(S(9))
    qtil = bf16v(S(7))[:, 0:4096]
    sstate = c.sb("p4_state", (128, H * DH), F32)
    sinb = bf16v(S(8))[:, 4352:4352 + H * DH]
    stmp = S(8)[0:64, 0:H * DH]
    tmpsq = f32v(S(8))[:, 1024:1536]
    gcol_sb = f32v(S(8))[:, 2688:2688 + NCH * 32]
    accS = c.sb("p5_accS", (128, NCH), F32)
    accQ = c.sb("p5_accQ", (128, NCH), F32)
    gnsb = c.sb("p5_gn", (16, 2), F32)
    zmask_sb = c.sb("zmask_sb", (128, 1), F32)
    blk1_sb = c.sb("blk1_sb", (128, 128), F32)

    psC = [c.ps("ps0"), c.ps("ps1")]
    psG = c.ps("ps2")

    b.dma("sp", gcol_sb, gcorr.ap(), writes=(S(8),), tag="p5 gc")
    b.op("dve", lambda e: e.memset(accS[:], 0.0), writes=(accS,), tag="p5 a0")
    b.op("dve", lambda e: e.memset(accQ[:], 0.0), writes=(accQ,), tag="p5 a1")
    load_reduced(c, d_st_in(c), d_st_out(c), d_st_pair(c), sstate[0:64, :], sstate,
                 stmp, S(8), (64, H * DH), "p5 sld")
    b.op("dve", lambda e: e.tensor_tensor(
        sstate[0:64, :], sstate[0:64, :],
        zmask_sb[0:64, 0:1].broadcast_to([64, H * DH]), ALU.mult),
        reads=(sstate, zmask_sb), writes=(sstate,), tag="p5 sz")
    b.dma("pl", sstate[64:128, :], sstate[0:64, :], reads=(sstate,), writes=(sstate,),
          tag="p5 srep")
    for hh in range(H):
        b.op("dve", lambda e, hh=hh: e.tensor_scalar(
            sinb[:, hh * DH:(hh + 1) * DH], sstate[:, hh * DH:(hh + 1) * DH],
            float(GAMMA[hh]), None, ALU.mult),
            reads=(sstate,), writes=(S(8),), tag="p5 sb")

    for ti in range(NTT):
        t0 = ti * TT
        b.dma("sp", ytile.rearrange("p (ch t) -> p ch t", ch=NCH),
              fm_ap(yloc)[:, :, t0:t0 + TT], writes=(S(9),), tag="p5 ldy")
        b.dma("sp", qtil.rearrange("p (ch t) -> p ch t", ch=NCH),
              fm_ap(qtb)[:, :, t0:t0 + TT], writes=(S(7),), tag="p5 ldq")
        for cc in range(4):
            cg = ti * 4 + cc
            for hh in range(H):
                par, fch = hh % 2, hh // 2
                pb = psC[fch % 2]
                b.op("pe", lambda e, hh=hh, par=par, fch=fch, cc=cc, pb=pb: e.matmul(
                    pb[par * 64:(par + 1) * 64, 0:128],
                    sinb[par * 64:(par + 1) * 64, hh * DH:(hh + 1) * DH],
                    qtil[par * 64:(par + 1) * 64,
                         fch * TT + cc * 128: fch * TT + cc * 128 + 128],
                    start=(par == 0), stop=(par == 1)),
                    reads=(S(8), S(7)), writes=(pb,), tag="p5 corr")
                if par == 1:
                    b.op("dve", lambda e, fch=fch, cc=cc, cg=cg, pb=pb: e.scalar_tensor_tensor(
                        ytile[:, fch * TT + cc * 128: fch * TT + cc * 128 + 128],
                        pb[:, 0:128], gcol_sb[:, fch * 32 + cg: fch * 32 + cg + 1],
                        ytile[:, fch * TT + cc * 128: fch * TT + cc * 128 + 128],
                        ALU.mult, ALU.add),
                        reads=(pb, S(8), S(9)), writes=(S(9),), tag="p5 yadd")
        for ch in range(NCH):
            sl = slice(ch * TT, (ch + 1) * TT)
            b.op("dve", lambda e, ch=ch, sl=sl: e.reduce_sum(
                tmpsq[:, 0:1], ytile[:, sl], axis=AX),
                reads=(S(9),), writes=(S(8),), tag="p5 rs")
            b.op("dve", lambda e, ch=ch: e.tensor_tensor(
                accS[:, ch:ch + 1], accS[:, ch:ch + 1], tmpsq[:, 0:1], ALU.add),
                reads=(accS, S(8)), writes=(accS,), tag="p5 as")
            b.op("act", lambda e, sl=sl: e.activation(
                tmpsq[:], ytile[:, sl], AF.Square),
                reads=(S(9),), writes=(S(8),), tag="p5 sq")
            b.op("dve", lambda e, sl=sl: e.reduce_sum(
                tmpsq[:, 0:1], tmpsq[:], axis=AX),
                reads=(S(8),), writes=(S(8),), tag="p5 rs2")
            b.op("dve", lambda e, ch=ch: e.tensor_tensor(
                accQ[:, ch:ch + 1], accQ[:, ch:ch + 1], tmpsq[:, 0:1], ALU.add),
                reads=(accQ, S(8)), writes=(accQ,), tag="p5 aq")
        b.dma("sp", fm_ap(yfin)[:, :, t0:t0 + TT],
              ytile.rearrange("p (ch t) -> p ch t", ch=NCH),
              reads=(S(9),), writes=(yfin,), tag="p5 sty")
    for ch in range(NCH):
        b.op("pe", lambda e, ch=ch: e.matmul(
            psG[0:16, 0:1], blk1_sb[:, ch * 16:(ch + 1) * 16], accS[:, ch:ch + 1],
            start=(ch == 0), stop=(ch == NCH - 1)),
            reads=(blk1_sb, accS), writes=(psG,), tag="p5 hs")
    for ch in range(NCH):
        b.op("pe", lambda e, ch=ch: e.matmul(
            psG[0:16, 1:2], blk1_sb[:, ch * 16:(ch + 1) * 16], accQ[:, ch:ch + 1],
            start=(ch == 0), stop=(ch == NCH - 1)),
            reads=(blk1_sb, accQ), writes=(psG,), tag="p5 hq")
    b.op("dve", lambda e: e.tensor_copy(gnsb[:], psG[0:16, 0:2]),
         reads=(psG,), writes=(gnsb,), tag="p5 gev")
    b.dma("sp", gn_in.ap(), gnsb[:], reads=(gnsb,), writes=(gn_in,), tag="p5 stgn")


def phx3(c):
    b = c.b
    b.phase("phx3")
    if c.fused:
        exchange(c, d_gn_in(c), d_gn_out(c), "x3 gn")



def ph6(c):
    b = c.b
    b.phase("ph6")
    yfin = d_yfin(c); xn1b = d_xn1b(c); x1fm = d_x1fm(c); out = d_out(c)
    wg = c.dram("w_gatedT", (D, D), F32)
    wp = c.dram("w_projT", (D, D), F32)
    wr = c.dram("W_rT", (D, D), F32)
    wk = c.dram("W_kT", (D, D), F32)
    wv = c.dram("W_vT", (D, D), F32)

    S = c.slot
    GN_N = float(DH * 8192)

    wg_sb = bf16v(S(0)); wp_sb = bf16v(S(1))
    wr_sb = bf16v(S(2)); wk_sb = bf16v(S(3)); wv_sb = bf16v(S(6))
    ytile = f32v(S(9))
    xnt = bf16v(S(4))[:, 0:4096]
    ztile = bf16v(S(4))[:, 4096:8192]
    yn_t = f32v(S(10))
    gate_t = f32v(S(5))
    prev = bf16v(S(11))[:, 0:4096]
    xrk = bf16v(S(11))[:, 4096:8192]
    outst = f32v(S(11))
    gnc = c.sb("p6_gnc", (128, 16), F32)
    gnsb = c.sb("p5_gn", (16, 2), F32)
    gntmp = c.sb("p6_gnt", (16, 2), F32)
    hcol = c.sb("p6_hcol", (128, 8), BF16)
    halo_f = c.sb("halo_sb", (128, 8), F32)
    halo_tmp = f32v(S(8))[:, 1536:1544]
    zmask_sb = c.sb("zmask_sb", (128, 1), F32)
    repsel_sb2 = f32v(S(8))[0:16, 512:1536]
    id32_sb = c.sb("id32_sb", (128, 128), F32)

    ps0 = c.ps("ps0"); ps1 = c.ps("ps1")
    ps2 = c.ps("ps2"); ps3 = c.ps("ps3")
    ps4 = c.ps("ps4")

    gnw_col = load_colN(c, "gnw", "L1_attn_gn_weight", 8, ps4)
    gnb_col = load_colN(c, "gnb", "L1_attn_gn_bias", 8, ps4)
    mur_col = load_colN(c, "mur", "L1_ffn_mu_r", 8, ps4)
    muk_col = load_colN(c, "muk", "L1_ffn_mu_k", 8, ps4)
    omr_col = c.sb("p6_omr", (128, 8), F32)
    omk_col = c.sb("p6_omk", (128, 8), F32)
    b.op("dve", lambda e: e.tensor_scalar(omr_col[:], mur_col[:], -1.0, 1.0,
                                          ALU.mult, ALU.add),
         reads=(mur_col,), writes=(omr_col,), tag="p6 omr")
    b.op("dve", lambda e: e.tensor_scalar(omk_col[:], muk_col[:], -1.0, 1.0,
                                          ALU.mult, ALU.add),
         reads=(muk_col,), writes=(omk_col,), tag="p6 omk")
    for wname, dram, wkey, wsb in (("g", wg, S(0), wg_sb), ("p", wp, S(1), wp_sb),
                                   ("r", wr, S(2), wr_sb), ("k", wk, S(3), wk_sb),
                                   ("v", wv, S(6), wv_sb)):
        b.dma("pl", wsb.rearrange("p (ci o) -> p ci o", ci=NCH),
              dram.ap().rearrange("(ci p) o -> p ci o", p=128),
              writes=(wkey,), tag="p6 w" + wname)
    b.dma("sp", repsel_sb2, d_repsel(c).ap(), writes=(S(8),), tag="p6 rp")

    load_reduced(c, d_gn_in(c), d_gn_out(c), d_gn_pair(c), gnsb[:], gnsb,
                 gntmp[:], gntmp, (16, 2), "p6 gnld")
    b.op("dve", lambda e: e.tensor_scalar(gnsb[:], gnsb[:], 1.0 / GN_N, None, ALU.mult),
         reads=(gnsb,), writes=(gnsb,), tag="p6 gn1")
    b.op("dve", lambda e: e.tensor_tensor(gntmp[:, 0:1], gnsb[:, 0:1], gnsb[:, 0:1],
                                          ALU.mult),
         reads=(gnsb,), writes=(gntmp,), tag="p6 gn2")
    b.op("dve", lambda e: e.tensor_tensor(gntmp[:, 1:2], gnsb[:, 1:2], gntmp[:, 0:1],
                                          ALU.subtract),
         reads=(gnsb, gntmp), writes=(gntmp,), tag="p6 gn3")
    b.op("dve", lambda e: e.tensor_scalar_add(gntmp[:, 1:2], gntmp[:, 1:2], EPS_LN),
         reads=(gntmp,), writes=(gntmp,), tag="p6 gn4")
    b.op("act", lambda e: e.activation(gntmp[:, 1:2], gntmp[:, 1:2], AF.Ln),
         reads=(gntmp,), writes=(gntmp,), tag="p6 gn5")
    b.op("act", lambda e: e.activation(gnsb[:, 1:2], gntmp[:, 1:2], AF.Exp, scale=-0.5),
         reads=(gntmp,), writes=(gnsb,), tag="p6 gn6")
    for ch in range(NCH):
        b.op("pe", lambda e, ch=ch: e.matmul(
            ps4[0:128, 0:2], repsel_sb2[:, ch * 128:(ch + 1) * 128], gnsb[:],
            start=True, stop=True),
            reads=(S(8), gnsb), writes=(ps4,), tag="p6 gcol")
        b.op("dve", lambda e, ch=ch: e.tensor_copy(gnc[:, 2 * ch:2 * ch + 2],
                                                   ps4[0:128, 0:2]),
             reads=(ps4,), writes=(gnc,), tag="p6 gcc")
    load_reduced(c, d_halo_in(c), d_halo_out(c), d_halo_pair(c), halo_f[:], halo_f,
                 halo_tmp, S(8), (128, 8), "p6 hld")
    b.op("dve", lambda e: e.tensor_tensor(
        halo_f[:], halo_f[:], zmask_sb[:, 0:1].broadcast_to([128, 8]), ALU.mult),
        reads=(halo_f, zmask_sb), writes=(halo_f,), tag="p6 hz")
    b.op("dve", lambda e: e.tensor_copy(hcol[:], halo_f[:]),
         reads=(halo_f,), writes=(hcol,), tag="p6 hc")

    for ti in range(NTT):
        t0 = ti * TT
        b.dma("sp", ytile.rearrange("p (ch t) -> p ch t", ch=NCH),
              fm_ap(yfin)[:, :, t0:t0 + TT], writes=(S(9),), tag="p6 ldy")
        b.dma("sp", xnt.rearrange("p (ch t) -> p ch t", ch=NCH),
              fm_ap(xn1b)[:, :, t0:t0 + TT], writes=(S(4),), tag="p6 ldxn")
        if ti == 0:
            b.dma("sp", prev.rearrange("p (ch t) -> p ch t", ch=NCH)[:, :, 1:TT],
                  fm_ap(xn1b)[:, :, 0:TT - 1], writes=(S(11),), tag="p6 ldpv0")
            for ch in range(NCH):
                b.op("dve", lambda e, ch=ch: e.tensor_copy(
                    prev[:, ch * TT: ch * TT + 1], hcol[:, ch:ch + 1]),
                    reads=(hcol,), writes=(S(11),), tag="p6 pv0")
        else:
            b.dma("sp", prev.rearrange("p (ch t) -> p ch t", ch=NCH),
                  fm_ap(xn1b)[:, :, t0 - 1:t0 + TT - 1], writes=(S(11),), tag="p6 ldpv")
        for ch in range(NCH):
            sl = slice(ch * TT, (ch + 1) * TT)
            b.op("dve", lambda e, sl=sl, ch=ch: e.tensor_scalar(
                yn_t[:, sl], ytile[:, sl], gnc[:, 2 * ch:2 * ch + 1],
                gnc[:, 2 * ch + 1:2 * ch + 2], ALU.subtract, ALU.mult),
                reads=(S(9), gnc), writes=(S(10),), tag="p6 yn1")
            b.op("dve", lambda e, sl=sl, ch=ch: e.tensor_scalar(
                yn_t[:, sl], yn_t[:, sl], gnw_col[:, ch:ch + 1], gnb_col[:, ch:ch + 1],
                ALU.mult, ALU.add),
                reads=(S(10), gnw_col, gnb_col), writes=(S(10),), tag="p6 yn2")
        for mc in range(NCH):
            pb = ps0 if mc % 2 == 0 else ps1
            for ci in range(NCH):
                b.op("pe", lambda e, ci=ci, mc=mc, pb=pb: e.matmul(
                    pb[:, 0:TT], wg_sb[:, ci * D + mc * 128: ci * D + mc * 128 + 128],
                    xnt[:, ci * TT:(ci + 1) * TT],
                    start=(ci == 0), stop=(ci == NCH - 1)),
                    reads=(S(0), S(4)), writes=(pb,), tag="p6 gt")
            b.op("act", lambda e, mc=mc, pb=pb: e.activation(
                gate_t[:, mc * TT:(mc + 1) * TT], pb[:, 0:TT], AF.Silu),
                reads=(pb,), writes=(S(5),), tag="p6 silu")
            b.op("dve", lambda e, mc=mc: e.tensor_tensor(
                ztile[:, mc * TT:(mc + 1) * TT], gate_t[:, mc * TT:(mc + 1) * TT],
                yn_t[:, mc * TT:(mc + 1) * TT], ALU.mult),
                reads=(S(5), S(10)), writes=(S(4),), tag="p6 z")
        for mc in range(NCH):
            pb = ps0 if mc % 2 == 0 else ps1
            for ci in range(NCH):
                b.op("pe", lambda e, ci=ci, mc=mc, pb=pb: e.matmul(
                    pb[:, 0:TT], wp_sb[:, ci * D + mc * 128: ci * D + mc * 128 + 128],
                    ztile[:, ci * TT:(ci + 1) * TT],
                    start=(ci == 0), stop=(ci == NCH - 1)),
                    reads=(S(1), S(4)), writes=(pb,), tag="p6 pr")
            b.op("dve", lambda e, mc=mc, pb=pb: e.tensor_copy(
                gate_t[:, mc * TT:(mc + 1) * TT], pb[:, 0:TT]),
                reads=(pb,), writes=(S(5),), tag="p6 sev")
        for mix, mcol, ocol in (("r", mur_col, omr_col), ("k", muk_col, omk_col)):
            for ch in range(NCH):
                sl = slice(ch * TT, (ch + 1) * TT)
                b.op("dve", lambda e, sl=sl, ch=ch, mcol=mcol: e.tensor_scalar(
                    f32v(S(8))[:, 0:TT], xnt[:, sl], mcol[:, ch:ch + 1], None, ALU.mult),
                    reads=(S(4), mcol), writes=(S(8),), tag="p6 mx1" + mix)
                b.op("dve", lambda e, sl=sl, ch=ch, ocol=ocol: e.scalar_tensor_tensor(
                    xrk[:, sl], prev[:, sl], ocol[:, ch:ch + 1], f32v(S(8))[:, 0:TT],
                    ALU.mult, ALU.add),
                    reads=(S(11), ocol, S(8)), writes=(S(11),), tag="p6 mx2" + mix)
            if mix == "r":
                for mc in range(NCH):
                    pb = ps2 if mc % 2 == 0 else ps3
                    for ci in range(NCH):
                        b.op("pe", lambda e, ci=ci, mc=mc, pb=pb: e.matmul(
                            pb[:, 0:TT],
                            wr_sb[:, ci * D + mc * 128: ci * D + mc * 128 + 128],
                            xrk[:, ci * TT:(ci + 1) * TT],
                            start=(ci == 0), stop=(ci == NCH - 1)),
                            reads=(S(2), S(11)), writes=(pb,), tag="p6 r")
                    b.op("act", lambda e, mc=mc, pb=pb: e.activation(
                        yn_t[:, mc * TT:(mc + 1) * TT], pb[:, 0:TT], AF.Sigmoid),
                        reads=(pb,), writes=(S(10),), tag="p6 sig")
            else:
                for mc in range(NCH):
                    pb = ps2 if mc % 2 == 0 else ps3
                    for ci in range(NCH):
                        b.op("pe", lambda e, ci=ci, mc=mc, pb=pb: e.matmul(
                            pb[:, 0:TT],
                            wk_sb[:, ci * D + mc * 128: ci * D + mc * 128 + 128],
                            xrk[:, ci * TT:(ci + 1) * TT],
                            start=(ci == 0), stop=(ci == NCH - 1)),
                            reads=(S(3), S(11)), writes=(pb,), tag="p6 k")
                    b.op("dve", lambda e, mc=mc, pb=pb: e.tensor_scalar_max(
                        f32v(S(8))[:, 0:TT], pb[:, 0:TT], 0.0),
                        reads=(pb,), writes=(S(8),), tag="p6 krelu")
                    b.op("act", lambda e, mc=mc: e.activation(
                        prev[:, mc * TT:(mc + 1) * TT], f32v(S(8))[:, 0:TT], AF.Square),
                        reads=(S(8),), writes=(S(11),), tag="p6 k2")
        for mc in range(NCH):
            pb = ps2 if mc % 2 == 0 else ps3
            for ci in range(NCH):
                b.op("pe", lambda e, ci=ci, mc=mc, pb=pb: e.matmul(
                    pb[:, 0:TT], wv_sb[:, ci * D + mc * 128: ci * D + mc * 128 + 128],
                    prev[:, ci * TT:(ci + 1) * TT],
                    start=(ci == 0), stop=(ci == NCH - 1)),
                    reads=(S(6), S(11)), writes=(pb,), tag="p6 f")
            b.op("dve", lambda e, mc=mc, pb=pb: e.tensor_tensor(
                f32v(S(8))[:, 0:TT], pb[:, 0:TT], yn_t[:, mc * TT:(mc + 1) * TT],
                ALU.mult),
                reads=(pb, S(10)), writes=(S(8),), tag="p6 rf")
            b.op("act", lambda e, mc=mc: e.activation(
                yn_t[:, mc * TT:(mc + 1) * TT], f32v(S(8))[:, 0:TT], AF.Silu),
                reads=(S(8),), writes=(S(10),), tag="p6 g1")
        b.dma("sp", ytile.rearrange("p (ch t) -> p ch t", ch=NCH),
              fm_ap(x1fm)[:, :, t0:t0 + TT], writes=(S(9),), tag="p6 ldx1")
        for ch in range(NCH):
            sl = slice(ch * TT, (ch + 1) * TT)
            b.op("dve", lambda e, sl=sl: e.tensor_tensor(
                gate_t[:, sl], gate_t[:, sl], yn_t[:, sl], ALU.mult),
                reads=(S(5), S(10)), writes=(S(5),), tag="p6 sg")
            b.op("dve", lambda e, sl=sl: e.tensor_tensor(
                ytile[:, sl], ytile[:, sl], gate_t[:, sl], ALU.add),
                reads=(S(9), S(5)), writes=(S(9),), tag="p6 x2")
        for i in range(4):
            pst = ps2 if i % 2 == 0 else ps3
            for ch in range(NCH):
                b.op("pe", lambda e, i=i, ch=ch, pst=pst: e.transpose(
                    pst[:, (ch % 4) * 128:(ch % 4) * 128 + 128],
                    ytile[:, ch * TT + i * 128: ch * TT + i * 128 + 128], id32_sb[:]),
                    reads=(S(9), id32_sb), writes=(pst,), tag="p6 oT")
                b.op("dve", lambda e, i=i, ch=ch, pst=pst: e.tensor_copy(
                    outst[:, i * D + ch * 128: i * D + ch * 128 + 128],
                    pst[:, (ch % 4) * 128:(ch % 4) * 128 + 128]),
                    reads=(pst,), writes=(S(11),), tag="p6 oc")
        b.dma("sp", out.ap()[t0:t0 + TT, :].rearrange("(s p) f -> p s f", p=128),
              outst.rearrange("p (s f) -> p s f", s=4),
              reads=(S(11),), writes=(out,), tag="p6 sto")



def _host_constants(h):
    con = {}
    con["c_id32"] = np.eye(128, dtype=np.float32)
    con["c_idb"] = np.eye(128, dtype=ml_dtypes.bfloat16)
    blk = np.zeros((128, 128), np.float32)
    for cch in range(8):
        for p in range(128):
            blk[p, 16 * cch + 2 * cch + p // 64] = 1.0
    blk = np.zeros((128, 128), np.float32)
    for cch in range(8):
        for p in range(128):
            blk[p, 16 * cch + (2 * cch + p // 64)] = 1.0
    con["c_blk1"] = blk
    rep = np.zeros((16, 1024), np.float32)
    for cch in range(8):
        for p in range(128):
            rep[2 * cch + p // 64, 128 * cch + p] = 1.0
    con["c_repsel"] = rep
    sel = np.zeros((8, 1024), np.float32)
    for e in range(8):
        sel[e, 128 * e:128 * (e + 1)] = 1.0
    con["c_sel"] = sel
    p2 = np.zeros((128, 128), np.float32)
    for hh in range(2):
        o = 64 * hh
        for i in range(32):
            p2[o + 2 * i + 1, o + 2 * i] = -1.0
            p2[o + 2 * i, o + 2 * i + 1] = 1.0
    con["c_p2"] = p2
    cau = np.zeros((128, 128), np.float32)
    for j in range(128):
        cau[j, j:] = 1.0
    con["c_causal"] = cau
    con["c_zmask"] = np.full((128, 1), float(h), np.float32)
    gcorr = np.zeros((128, 8 * 32), np.float64)
    for fch in range(8):
        for cg in range(32):
            for p in range(128):
                hh = 2 * fch + p // 64
                gcorr[p, fch * 32 + cg] = GAMMA[hh] ** (128.0 * cg)
    con["c_gcorr"] = gcorr.astype(np.float32)

    T_FULL = 8192
    d = DH
    half = np.arange(0, d, 2, dtype=np.float64)
    inv_freq = 1.0 / (10000.0 ** (half / d))
    tloc = np.arange(TLOC, dtype=np.float64) + h * TLOC
    freqs = np.repeat(tloc[:, None] * inv_freq[None, :], 2, axis=1)
    scale_vec = (half + 0.4 * d) / (1.4 * d)
    power = (tloc - T_FULL // 2) / 512.0
    scale = np.repeat(scale_vec[None, :] ** power[:, None], 2, axis=1)
    cosf, sinf = np.cos(freqs), np.sin(freqs)
    i_loc = (np.arange(TLOC) % LCH).astype(np.float64)
    Aq = np.zeros((D, TLOC), np.float64)
    Bq = np.zeros((D, TLOC), np.float64)
    Ak = np.zeros((D, TLOC), np.float64)
    Bk = np.zeros((D, TLOC), np.float64)
    s8 = 1.0 / np.sqrt(8.0)
    for hh in range(H):
        g = GAMMA[hh]
        gq = np.power(g, i_loc) * (1.0 / 8.0)
        gk = np.power(g, -i_loc)
        Aq[hh * 64:(hh + 1) * 64, :] = (cosf * scale).T * gq[None, :]
        Bq[hh * 64:(hh + 1) * 64, :] = (sinf * scale).T * gq[None, :]
        Ak[hh * 64:(hh + 1) * 64, :] = (cosf / scale).T * gk[None, :]
        Bk[hh * 64:(hh + 1) * 64, :] = (sinf / scale).T * gk[None, :]
    con["c_ropeAq"] = Aq.astype(np.float32)
    con["c_ropeBq"] = Bq.astype(np.float32)
    con["c_ropeAk"] = Ak.astype(np.float32)
    con["c_ropeBk"] = Bk.astype(np.float32)
    return con


def _host_inputs_per_core(inputs):
    x_full = np.ascontiguousarray(np.asarray(inputs["L0_attn_src"], dtype=np.float32))
    weights = {}
    for k, v in inputs.items():
        if k == "L0_attn_src":
            continue
        a = np.asarray(v, dtype=np.float32)
        weights[k] = np.ascontiguousarray(a)
    weights["w_qkvT"] = np.ascontiguousarray(weights["L1_attn_w_qkv"].T)
    weights["w_gatedT"] = np.ascontiguousarray(weights["L1_attn_w_gated"].T)
    weights["w_projT"] = np.ascontiguousarray(weights["L1_attn_w_proj"].T)
    weights["W_rT"] = np.ascontiguousarray(weights["L1_ffn_W_r"].T)
    weights["W_kT"] = np.ascontiguousarray(weights["L1_ffn_W_k"].T)
    weights["W_vT"] = np.ascontiguousarray(weights["L1_ffn_W_v"].T)
    cons = [_host_constants(0), _host_constants(1)]
    maps = []
    for core in range(8):
        bb, hh = core // 2, core % 2
        m = {"x": np.ascontiguousarray(x_full[bb, hh * TLOC:(hh + 1) * TLOC, :])}
        m.update(weights)
        m.update(cons[hh])
        maps.append(m)
    return maps



ALL_PHASES = [ph0, ph1, phx1, ph2, ph3, ph4, phx2, ph5, phx3, ph6]


def build_nc(phases, ext_in, ext_out, fused):
    nc = bass.Bass("TRN2", target_bir_lowering=False)
    b = Builder()
    c = Ctx(nc, b, ext_in, ext_out, fused)
    for ph in phases:
        ph(c)
    b.finalize_deps()
    b.render(nc)
    nc.finalize()
    return nc, c


_CACHE = {}


def kernel(**inputs):
    maps = _host_inputs_per_core(inputs)
    ext_in = set(maps[0].keys())
    if "nc" not in _CACHE:
        nc, c = build_nc(ALL_PHASES, ext_in, {"o"}, fused=True)
        used = set(c.dram_t.keys())
        _CACHE["nc"] = nc
        _CACHE["used"] = used
    nc, used = _CACHE["nc"], _CACHE["used"]
    maps = [{k: v for k, v in m.items() if k in used} for m in maps]
    res = run_bass_kernel_spmd(nc, maps, core_ids=list(range(8)))
    out = np.empty((4, 8192, 1024), np.float32)
    for core in range(8):
        bb, hh = core // 2, core % 2
        out[bb, hh * TLOC:(hh + 1) * TLOC] = res.results[core]["o"]
    return out
